# version 06
# baseline (speedup 1.0000x reference)
import numpy as np
import ml_dtypes
import concourse.bass as bass
import concourse.mybir as mybir
from concourse.bass_utils import run_bass_kernel_spmd
from contextlib import ExitStack

F32 = mybir.dt.float32
BF16 = mybir.dt.bfloat16
ALU = mybir.AluOpType
ACT = mybir.ActivationFunctionType
AX = mybir.AxisListType

D = 4096
L2 = 2048
LQ = 1024
NH = 16
HD = 256
I = 11008
NTILE = L2 // 128
NOWN = LQ // 128
EPS = 1e-5


class Eng:
    def __init__(self, eng, sem):
        self.e = eng
        self.sem = sem
        self.n = 0

    def op(self, method, *args, **kw):
        if self.n > 0:
            self.e.wait_ge(self.sem, self.n)
        inst = getattr(self.e, method)(*args, **kw)
        inst.then_inc(self.sem, 1)
        self.n += 1
        return inst

    def wait(self, sem, count):
        if count > 0:
            self.e.wait_ge(sem, count)


class Stream:
    def __init__(self, q, sem):
        self.q = q
        self.sem = sem
        self.cnt = 0

    def dma(self, out, in_):
        self.q.dma_start(out=out, in_=in_).then_inc(self.sem, 16)
        self.cnt += 16
        return self.cnt

    def dmaT(self, out, in_):
        self.q.dma_start_transpose(out=out, in_=in_).then_inc(self.sem, 16)
        self.cnt += 16
        return self.cnt


def _build_tmats():
    g = [1.0, 1.0, 1.0]
    dim, n = 3, 8
    masks = sorted(range(n), key=lambda m: (bin(m).count('1'), m))
    inv = {m: i for i, m in enumerate(masks)}
    C = np.zeros((n, n, n), dtype=np.float64)
    for i, a in enumerate(masks):
        for j, b in enumerate(masks):
            sw, at = 0, a >> 1
            while at:
                sw += bin(at & b).count('1')
                at >>= 1
            s = -1.0 if sw & 1 else 1.0
            for t in range(dim):
                if a >> t & 1 and b >> t & 1:
                    s *= g[t]
            C[i, j, inv[a ^ b]] = s
    grades = np.array([bin(m).count('1') for m in masks])
    rev = np.where(grades * (grades - 1) // 2 % 2 == 1, -1.0, 1.0)
    BIV = [4, 5, 6]
    out = np.zeros((128, 512), dtype=np.float32)
    for c in range(3):
        G = rev[:, None] * C[:, :, BIV[c]]
        for d in range(16):
            out[d*8:(d+1)*8, c*128 + d*8:c*128 + (d+1)*8] = G
    C0 = np.diag(rev / 16.0)
    for d in range(16):
        out[d*8:(d+1)*8, 384 + d*8:384 + (d+1)*8] = C0
    return out


def _blk3d(ap, b=32):
    return ap.rearrange("p (a b) -> p a b", b=b)


def _bcast(ap, nblk):
    return ap.unsqueeze(2).broadcast_to([128, nblk, 32])


def _emit_quant(v, s, src, cols, mbits, T, qout, qgate=None):
    nb = cols // 32
    am, b1, b2, b3, b4, b5 = (T[k] for k in ("am", "b1", "b2", "b3", "b4", "b5"))
    zt, at_, u2 = T["zt"], T["at"], T["u2"]
    cA, cB, cC, cD, cE, cF = (T[k] for k in ("cA", "cB", "cC", "cD", "cE", "cF"))
    sgn = T["sgn"]
    amc = am[:, :nb]; b1c = b1[:, :nb]; b2c = b2[:, :nb]
    b3c = b3[:, :nb]; b4c = b4[:, :nb]; b5c = b5[:, :nb]
    srcc = src[:, :cols]; ztc = zt[:, :cols]; atc = at_[:, :cols]
    u2c = u2[:, :cols]
    cAc = cA[:, :cols]; cBc = cB[:, :cols]; cCc = cC[:, :cols]
    cDc = cD[:, :cols]; cEc = cE[:, :cols]; cFc = cF[:, :cols]
    sgnc = sgn[:, :cols]; qoutc = qout[:, :cols]

    v.op("tensor_reduce", amc.unsqueeze(2), _blk3d(srcc), AX.X, ALU.max,
         apply_absolute_value=True)
    v.op("tensor_single_scalar", b1c, amc, 0.0, ALU.is_gt)
    v.op("tensor_scalar", b2c, b1c, -1.0, 1.0, ALU.mult, ALU.add)
    v.op("tensor_tensor", b3c, amc, b2c, ALU.add)
    v.op("tensor_single_scalar", b3c, b3c, float(2.0**32), ALU.mult)
    v.op("memset", b4c, 1.0)
    v.op("memset", b2c, 1.0)
    for k in (16, 16, 16, 8, 4, 2, 1):
        v.op("tensor_single_scalar", b1c, b3c, float(2.0**k), ALU.is_ge)
        v.op("tensor_scalar", b5c, b1c, float(2.0**k - 1.0), 1.0, ALU.mult, ALU.add)
        v.op("tensor_tensor", b2c, b2c, b5c, ALU.mult)
        v.op("tensor_scalar", b1c, b1c, float(2.0**-k - 1.0), 1.0, ALU.mult, ALU.add)
        v.op("tensor_tensor", b3c, b3c, b1c, ALU.mult)
        v.op("tensor_tensor", b4c, b4c, b1c, ALU.mult)
    v.op("tensor_single_scalar", b4c, b4c, float(2.0**34), ALU.mult)
    v.op("tensor_single_scalar", b5c, b2c, float(2.0**-34), ALU.mult)
    v.op("tensor_tensor", _blk3d(ztc), _blk3d(srcc), _bcast(b4c, nb), ALU.mult)
    zmark = v.n
    s.wait(v.sem, zmark)
    s.op("sign", sgnc, ztc)
    if mbits == 1:
        s.op("activation", atc, ztc, ACT.Abs)
    else:
        s.op("activation", atc, ztc, ACT.Abs, scale=8.0)
    amark = s.n
    v.wait(s.sem, amark)
    if mbits == 1:
        v.op("tensor_single_scalar", cAc, atc, 0.25, ALU.is_gt)
        v.op("scalar_tensor_tensor", cAc, atc, 0.75, cAc, ALU.is_ge, ALU.add)
        v.op("scalar_tensor_tensor", cAc, atc, 1.25, cAc, ALU.is_gt, ALU.add)
        v.op("scalar_tensor_tensor", cAc, atc, 1.75, cAc, ALU.is_ge, ALU.add)
        v.op("tensor_single_scalar", cBc, atc, 2.5, ALU.is_gt)
        v.op("scalar_tensor_tensor", cBc, atc, 3.5, cBc, ALU.is_ge, ALU.add)
        v.op("tensor_single_scalar", cCc, atc, 5.0, ALU.is_gt)
        v.op("scalar_tensor_tensor", cBc, cAc, 0.5, cBc, ALU.mult, ALU.add)
        v.op("scalar_tensor_tensor", cBc, cCc, 2.0, cBc, ALU.mult, ALU.add)
        v.op("tensor_tensor", cBc, cBc, sgnc, ALU.mult)
        if qgate is not None:
            qgate()
        v.op("tensor_tensor", _blk3d(qoutc), _blk3d(cBc), _bcast(b5c, nb), ALU.mult)
    else:
        v.op("tensor_single_scalar", cAc, atc, 16.0, ALU.is_ge)
        v.op("tensor_single_scalar", cBc, atc, 32.0, ALU.is_ge)
        v.op("tensor_scalar", cCc, cAc, -0.5, 1.0, ALU.mult, ALU.add)
        v.op("scalar_tensor_tensor", cCc, cBc, -0.25, cCc, ALU.mult, ALU.add)
        v.op("tensor_tensor", u2c, atc, cCc, ALU.mult)
        v.op("tensor_single_scalar", cDc, u2c, 8.0, ALU.is_ge)
        v.op("scalar_tensor_tensor", u2c, cDc, -8.0, u2c, ALU.mult, ALU.add)
        v.op("tensor_single_scalar", cEc, u2c, 4.0, ALU.is_ge)
        v.op("scalar_tensor_tensor", u2c, cEc, -4.0, u2c, ALU.mult, ALU.add)
        v.op("tensor_single_scalar", cCc, u2c, 0.5, ALU.is_gt)
        v.op("scalar_tensor_tensor", cCc, u2c, 1.5, cCc, ALU.is_ge, ALU.add)
        v.op("scalar_tensor_tensor", cCc, u2c, 2.5, cCc, ALU.is_gt, ALU.add)
        v.op("scalar_tensor_tensor", cCc, u2c, 3.5, cCc, ALU.is_ge, ALU.add)
        v.op("scalar_tensor_tensor", cCc, cEc, 4.0, cCc, ALU.mult, ALU.add)
        v.op("scalar_tensor_tensor", cCc, cDc, 8.0, cCc, ALU.mult, ALU.add)
        v.op("tensor_scalar", cFc, cBc, 2.0, 1.0, ALU.mult, ALU.add)
        v.op("tensor_tensor", cFc, cFc, cAc, ALU.add)
        v.op("scalar_tensor_tensor", cDc, cCc, 0.125, cFc, ALU.mult, ALU.mult)
        v.op("tensor_single_scalar", cDc, cDc, 7.5, ALU.min)
        v.op("tensor_tensor", cDc, cDc, sgnc, ALU.mult)
        if qgate is not None:
            qgate()
        v.op("tensor_tensor", _blk3d(qoutc), _blk3d(cDc), _bcast(b5c, nb), ALU.mult)


def build_nc():
    nc = bass.Bass("TRN2", target_bir_lowering=False)
    xb = nc.dram_tensor("xb", [L2, D], F32, kind="ExternalInput").ap()
    wqT = nc.dram_tensor("wqT", [D, D], F32, kind="ExternalInput").ap()
    wkT = nc.dram_tensor("wkT", [D, D], F32, kind="ExternalInput").ap()
    wvT = nc.dram_tensor("wvT", [D, D], F32, kind="ExternalInput").ap()
    bq_cm = nc.dram_tensor("bq_cm", [128, 32], F32, kind="ExternalInput").ap()
    bk_cm = nc.dram_tensor("bk_cm", [128, 32], F32, kind="ExternalInput").ap()
    bvf = nc.dram_tensor("bvf", [128, D], F32, kind="ExternalInput").ap()
    bwf = nc.dram_tensor("bwf", [128, 48], F32, kind="ExternalInput").ap()
    tm = nc.dram_tensor("tm", [128, 512], F32, kind="ExternalInput").ap()
    idn = nc.dram_tensor("idn", [128, 128], BF16, kind="ExternalInput").ap()
    wg = nc.dram_tensor("wg", [I, D], F32, kind="ExternalInput").ap()
    wu = nc.dram_tensor("wu", [I, D], F32, kind="ExternalInput").ap()
    wd = nc.dram_tensor("wd", [D, I], F32, kind="ExternalInput").ap()
    y = nc.dram_tensor("y", [LQ, D], F32, kind="ExternalOutput").ap()
    s_xnT = nc.dram_tensor("s_xnT", [128, 32 * L2], BF16, kind="Internal").ap()
    s_xnqT = nc.dram_tensor("s_xnqT", [128, 32 * LQ], BF16, kind="Internal").ap()
    s_attn = nc.dram_tensor("s_attn", [LQ, D], F32, kind="Internal").ap()
    s_wgT = nc.dram_tensor("s_wgT", [D, I], BF16, kind="Internal").ap()
    s_wuT = nc.dram_tensor("s_wuT", [D, I], BF16, kind="Internal").ap()
    s_wdT = nc.dram_tensor("s_wdT", [I, D], BF16, kind="Internal").ap()
    s_wgq = nc.dram_tensor("s_wgq", [I, D], BF16, kind="Internal").ap()
    s_wuq = nc.dram_tensor("s_wuq", [I, D], BF16, kind="Internal").ap()

    ctx = ExitStack()
    with ctx:
        def sb(name, shape, dt=F32):
            return ctx.enter_context(nc.sbuf_tensor(name, shape, dt))

        def sem(name):
            return ctx.enter_context(nc.semaphore(name))

        vsem = sem("vsem"); ssem = sem("ssem"); tsem = sem("tsem")
        v = Eng(nc.vector, vsem)
        s = Eng(nc.scalar, ssem)
        t = Eng(nc.tensor, tsem)

        lds = Stream(nc.sync, sem("lds"))
        ldx = [Stream(nc.sync, sem("ldx%d" % i)) for i in range(2)]
        stt = [Stream(nc.gpsimd, sem("stt%d" % i)) for i in range(2)]
        ld2s = Stream(nc.sync, sem("ld2s"))
        ldw = [Stream(nc.sync, sem("ldw%d" % i)) for i in range(2)]
        ldc = [Stream(nc.sync, sem("ldc%d" % i)) for i in range(2)]
        sto = [Stream(nc.gpsimd, sem("sto%d" % i)) for i in range(2)]
        ld3 = [Stream(nc.sync, sem("ld3%d" % i)) for i in range(2)]
        st3 = [Stream(nc.gpsimd, sem("st3%d" % i)) for i in range(2)]
        ldq4 = Stream(nc.sync, sem("ldq4"))
        ldg = [Stream(nc.sync, sem("ldg%d" % i)) for i in range(2)]
        ldd = [Stream(nc.sync, sem("ldd%d" % i)) for i in range(2)]
        ldr = Stream(nc.sync, sem("ldr"))
        sty = [Stream(nc.gpsimd, sem("sty%d" % i)) for i in range(2)]
        qld = [Stream(nc.gpsimd, sem("qld%d" % i)) for i in range(2)]
        qst = [Stream(nc.gpsimd, sem("qst%d" % i)) for i in range(2)]

        def qwait(q, semh, val):
            if val > 0:
                q.wait_ge(semh, val)

        def barrier(marks):
            for q in (nc.sync, nc.gpsimd):
                for semh, val in marks:
                    qwait(q, semh, val)
            for e in (v, s, t):
                for semh, val in marks:
                    e.wait(semh, val)

        idn_sb = sb("idn_sb", [128, 128], BF16)
        bq_sb = sb("bq_sb", [128, 32], F32)
        bk_sb = sb("bk_sb", [128, 32], F32)
        lnS = sb("lnS", [128, 1], F32); lnQ = sb("lnQ", [128, 1], F32)
        lnM = sb("lnM", [128, 1], F32); lnE = sb("lnE", [128, 1], F32)
        lnV = sb("lnV", [128, 1], F32); lnR = sb("lnR", [128, 1], F32)
        lnB = sb("lnB", [128, 1], F32)
        nm = sb("nm", [128, 4], F32); nmx = sb("nmx", [128, 1], F32)
        ac4 = sb("ac4", [128, 4], F32); ssum = sb("ssum", [128, 1], F32)
        rr = sb("rr", [128, 1], F32)

        QTG = {
            "am": sb("g_am", [128, 32], F32), "b1": sb("g_b1", [128, 32], F32),
            "b2": sb("g_b2", [128, 32], F32), "b3": sb("g_b3", [128, 32], F32),
            "b4": sb("g_b4", [128, 32], F32), "b5": sb("g_b5", [128, 32], F32),
            "zt": sb("g_zt", [128, 1024], F32), "at": sb("g_at", [128, 1024], F32),
            "u2": sb("g_u2", [128, 1024], F32),
            "cA": sb("g_cA", [128, 1024], BF16), "cB": sb("g_cB", [128, 1024], BF16),
            "cC": sb("g_cC", [128, 1024], BF16), "cD": sb("g_cD", [128, 1024], BF16),
            "cE": sb("g_cE", [128, 1024], BF16), "cF": sb("g_cF", [128, 1024], BF16),
            "sgn": sb("g_sgn", [128, 1024], BF16),
        }
        win2 = [sb("win2_%d" % i, [128, 1024], F32) for i in range(2)]
        qsg = [sb("qsg%d" % i, [128, 1024], BF16) for i in range(2)]

        wq_tiles = []
        for wdram_, wscr_ in ((wg, s_wgq), (wu, s_wuq)):
            for rt_ in range(I // 128):
                for cc_ in range(4):
                    wq_tiles.append((wdram_, wscr_, rt_, cc_))
        qstate = {"i": 0, "in_free": [0, 0], "st_mark": [0, 0]}

        def qtask(n):
            for _ in range(n):
                ti = qstate["i"]
                if ti >= len(wq_tiles):
                    return
                qstate["i"] += 1
                wdram_, wscr_, rt_, cc_ = wq_tiles[ti]
                par = ti % 2
                qwait(nc.gpsimd, vsem, qstate["in_free"][par])
                ldm = qld[par].dma(win2[par][:],
                                   wdram_[rt_*128:(rt_+1)*128, cc_*1024:(cc_+1)*1024])
                v.wait(qld[par].sem, ldm)
                s.wait(qld[par].sem, ldm)
                def qg(par=par):
                    v.wait(qst[par].sem, qstate["st_mark"][par])
                _emit_quant(v, s, win2[par][:], 1024, 1, QTG, qsg[par][:], qgate=qg)
                qstate["in_free"][par] = v.n
                qwait(nc.gpsimd, vsem, v.n)
                qst[par].dma(wscr_[rt_*128:(rt_+1)*128, cc_*1024:(cc_+1)*1024], qsg[par][:])
                qstate["st_mark"][par] = qst[par].cnt

        lds.dma(idn_sb[:], idn)
        lds.dma(bq_sb[:], bq_cm)
        lds.dma(bk_sb[:], bk_cm)
        small_ld = lds.cnt

        p1 = ExitStack()
        with p1:
            def sb1(name, shape, dt=F32):
                return p1.enter_context(nc.sbuf_tensor(name, shape, dt))
            xt2 = [sb1("xt2_%d" % i, [128, D], F32) for i in range(2)]
            sqscr = sb1("sqscr", [128, D], F32)
            xnf = sb1("xnf", [128, D], F32)
            xnb = sb1("xnb", [128, D], BF16)
            xnq_t = sb1("xnq_t", [128, D], BF16)
            tstage = [sb1("tstage%d" % i, [128, 1024], BF16) for i in range(2)]
            psT = [p1.enter_context(nc.psum_tensor("psT%d" % i, [128, 1024], BF16)) for i in range(2)]

            t.wait(lds.sem, small_ld)
            xnT_v = s_xnT.rearrange("p (a c) -> p a c", a=32)
            xnqT_v = s_xnqT.rearrange("p (a c) -> p a c", a=32)
            buf_free = [0, 0]
            pT_free = [0, 0]
            tstg_free = [0, 0]

            def emit_transpose_out(srcbuf, dst_view, tt):
                for half in range(4):
                    hb = half % 2
                    pt = psT[hb]
                    t.wait(vsem, pT_free[hb])
                    for k in range(8):
                        dc = half * 8 + k
                        t.op("transpose", pt[:, k*128:(k+1)*128], srcbuf[:, dc*128:(dc+1)*128], idn_sb[:])
                    tm_ = t.n
                    v.wait(tsem, tm_)
                    v.wait(stt[hb].sem, tstg_free[hb])
                    v.op("tensor_copy", tstage[hb][:], pt[:])
                    pT_free[hb] = v.n
                    em = v.n
                    qwait(nc.gpsimd, vsem, em)
                    stt[hb].dma(dst_view[:, half*8:(half+1)*8, tt*128:(tt+1)*128],
                                tstage[hb][:].rearrange("p (a c) -> p a c", a=8))
                    tstg_free[hb] = stt[hb].cnt

            for tt in range(NTILE):
                par = tt % 2
                qwait(nc.sync, ssem, buf_free[par])
                ldm = ldx[par].dma(xt2[par][:], xb[tt*128:(tt+1)*128, :])
                v.wait(ldx[par].sem, ldm)
                s.wait(ldx[par].sem, ldm)
                v.op("reduce_sum", lnS[:], xt2[par][:], AX.X)
                s.op("activation", sqscr[:], xt2[par][:], ACT.Square, accum_out=lnQ[:])
                sqm = s.n
                v.wait(ssem, sqm)
                v.op("tensor_single_scalar", lnM[:], lnS[:], 1.0 / D, ALU.mult)
                v.op("tensor_single_scalar", lnE[:], lnQ[:], 1.0 / D, ALU.mult)
                v.op("tensor_tensor", lnV[:], lnM[:], lnM[:], ALU.mult)
                v.op("tensor_tensor", lnV[:], lnE[:], lnV[:], ALU.subtract)
                v.op("tensor_single_scalar", lnV[:], lnV[:], EPS, ALU.add)
                vstatm = v.n
                s.wait(vsem, vstatm)
                s.op("activation", lnR[:], lnV[:], ACT.Sqrt)
                sq2 = s.n
                v.wait(ssem, sq2)
                v.op("reciprocal", lnR[:], lnR[:])
                v.op("scalar_tensor_tensor", lnB[:], lnM[:], -1.0, lnR[:], ALU.mult, ALU.mult)
                vstat2 = v.n
                s.wait(vsem, vstat2)
                s.op("activation", xnf[:], xt2[par][:], ACT.Identity, bias=lnB[:], scale=lnR[:])
                s.op("activation", xnb[:], xnf[:], ACT.Identity)
                buf_free[par] = s.n
                xnm = s.n
                t.wait(ssem, xnm)
                emit_transpose_out(xnb, xnT_v, tt)
                if tt < NOWN:
                    v.wait(ssem, xnm)
                    for cc in range(4):
                        _emit_quant(v, s, xnf[:, cc*1024:(cc+1)*1024], 1024, 3, QTG,
                                    xnq_t[:, cc*1024:(cc+1)*1024])
                    xqm = v.n
                    t.wait(vsem, xqm)
                    emit_transpose_out(xnq_t, xnqT_v, tt)
            p1_marks = [(vsem, v.n), (ssem, s.n), (tsem, t.n), (lds.sem, lds.cnt),
                        (ldx[0].sem, ldx[0].cnt), (ldx[1].sem, ldx[1].cnt),
                        (stt[0].sem, stt[0].cnt), (stt[1].sem, stt[1].cnt)]

        p2 = ExitStack()
        with p2:
            def sb2(name, shape, dt=F32):
                return p2.enter_context(nc.sbuf_tensor(name, shape, dt))
            wstage = [sb2("wstage%d" % i, [128, 2048], F32) for i in range(2)]
            tm_sb = sb2("tm_sb", [128, 512], F32)
            bw_sb = sb2("bw_sb", [128, 48], F32)
            bv_sb = sb2("bv_sb", [128, D], F32)
            th_all = sb2("th_all", [128, NH * 128], BF16)
            tmpf = sb2("tmpf", [128, 128], F32)
            wq_bf = sb2("wq_bf", [128, 32 * HD], BF16)
            wk_bf = sb2("wk_bf", [128, 32 * HD], BF16)
            wv_bf = sb2("wv_bf", [128, 32 * HD], BF16)
            xnc = [sb2("xnc%d" % i, [128, 512], BF16) for i in range(2)]
            kh = sb2("kh", [128, 2 * L2], BF16)
            vh = sb2("vh", [128, 16 * HD], BF16)
            qtmp = sb2("qtmp", [128, 2 * LQ], BF16)
            qeff = sb2("qeff", [128, 2 * LQ], BF16)
            attn = sb2("attn", [128, L2], BF16)
            attn_T = sb2("attn_T", [128, L2], BF16)
            ostg = [sb2("ostg%d" % i, [128, HD], F32) for i in range(2)]
            ps = [p2.enter_context(nc.psum_tensor("ps%d" % i, [128, 512], F32)) for i in range(6)]
            psQ = p2.enter_context(nc.psum_tensor("psQ", [128, 512], F32))
            psQ2 = p2.enter_context(nc.psum_tensor("psQ2", [128, 512], F32))

            barrier(p1_marks)
            ld2s.dma(tm_sb[:], tm)
            ld2s.dma(bw_sb[:], bwf)
            m2 = ld2s.dma(bv_sb[:], bvf)
            v.wait(ld2s.sem, m2)
            for h in range(NH):
                def bwap(c):
                    return bw_sb[:, h * 3 + c:h * 3 + c + 1]
                v.op("tensor_single_scalar", tmpf[:], tm_sb[:, 0:128], bwap(0), ALU.mult)
                v.op("scalar_tensor_tensor", tmpf[:], tm_sb[:, 128:256], bwap(1), tmpf[:], ALU.mult, ALU.add)
                v.op("scalar_tensor_tensor", tmpf[:], tm_sb[:, 256:384], bwap(2), tmpf[:], ALU.mult, ALU.add)
                v.op("tensor_tensor", th_all[:, h*128:(h+1)*128], tmpf[:], tm_sb[:, 384:512], ALU.add)

            wqT_v = wqT.rearrange("(a p) c -> p a c", p=128)
            wkT_v = wkT.rearrange("(a p) c -> p a c", p=128)
            wvT_v = wvT.rearrange("(a p) c -> p a c", p=128)

            wfree_v = [0, 0]
            wcast = {}
            wqi = 0
            xn_free = [0, 0]
            ldc_mark = [0, 0]
            pmark = {}

            for h in range(NH):
                t.wait(vsem, pmark.get(("trb", 0), 0))
                t.wait(vsem, pmark.get(("trb", 1), 0))
                t.wait(vsem, pmark.get("ostg_cons", 0))
                for wname, wview, wbuf in (("q", wqT_v, wq_bf), ("k", wkT_v, wk_bf), ("v", wvT_v, wv_bf)):
                    for quarter in range(4):
                        wp = wqi % 2
                        wqi += 1
                        qwait(nc.sync, vsem, wfree_v[wp])
                        ldm = ldw[wp].dma(wstage[wp][:].rearrange("p (a c) -> p a c", a=8),
                                          wview[:, quarter*8:(quarter+1)*8, h*HD:(h+1)*HD])
                        v.wait(ldw[wp].sem, ldm)
                        v.op("tensor_copy", wbuf[:, quarter*2048:(quarter+1)*2048], wstage[wp][:])
                        wfree_v[wp] = v.n
                    wcast[wname] = v.n
                t.wait(vsem, max(wcast.values()))
                qtask(4)
                for tc in range(4):
                    for dc in range(32):
                        par = dc % 2
                        qwait(nc.sync, tsem, xn_free[par])
                        ldc_mark[par] = ldc[par].dma(xnc[par][:],
                                                     s_xnT[:, dc*L2 + tc*512: dc*L2 + tc*512 + 512])
                        t.wait(ldc[par].sem, ldc_mark[par])
                        st = (dc == 0)
                        sp = (dc == 31)
                        for cc in range(2):
                            t.op("matmul", ps[cc][:], wk_bf[:, (dc*2+cc)*128:(dc*2+cc+1)*128],
                                 xnc[par][:], start=st, stop=sp)
                        for sub in range(4):
                            t.op("matmul", ps[2+sub][:, 0:HD], xnc[par][:, sub*128:(sub+1)*128],
                                 wv_bf[:, dc*HD:(dc+1)*HD], start=st, stop=sp)
                        if tc < 2:
                            for cc in range(2):
                                t.op("matmul", (psQ if cc == 0 else psQ2)[:],
                                     wq_bf[:, (dc*2+cc)*128:(dc*2+cc+1)*128],
                                     xnc[par][:], start=st, stop=sp)
                        xn_free[par] = t.n
                    prm = t.n
                    s.wait(tsem, prm)
                    s.wait(tsem, pmark.get("kh_cons", 0))
                    for cc in range(2):
                        s.op("activation", kh[:, cc*L2 + tc*512: cc*L2 + tc*512 + 512],
                             ps[cc][:], ACT.Identity, bias=bk_sb[:, h*2+cc:h*2+cc+1])
                    v.wait(tsem, prm)
                    for sub in range(4):
                        kc = tc * 4 + sub
                        v.op("tensor_tensor", vh[:, kc*HD:(kc+1)*HD], ps[2+sub][:, 0:HD],
                             bv_sb[:, h*HD:(h+1)*HD], ALU.add)
                    if tc < 2:
                        for cc in range(2):
                            s.op("activation", qtmp[:, cc*LQ + tc*512: cc*LQ + tc*512 + 512],
                                 (psQ if cc == 0 else psQ2)[:], ACT.Identity,
                                 bias=bq_sb[:, h*2+cc:h*2+cc+1])
                    evK = s.n
                    evV = v.n
                    t.wait(ssem, evK)
                    t.wait(vsem, evV)
                    qtask(3)
                for cc in range(2):
                    for nn in range(2):
                        t.op("matmul", ps[0][:], th_all[:, h*128:(h+1)*128],
                             qtmp[:, cc*LQ + nn*512: cc*LQ + nn*512 + 512], start=True, stop=True)
                        trm = t.n
                        s.wait(tsem, trm)
                        s.op("activation", qeff[:, cc*LQ + nn*512: cc*LQ + nn*512 + 512],
                             ps[0][:], ACT.Identity)
                        t.wait(ssem, s.n)
                qeffm = s.n
                t.wait(ssem, qeffm)
                qtask(2)
                for qt in range(8):
                    for kc in range(4):
                        for cc in range(2):
                            t.op("matmul", ps[kc][:],
                                 qeff[:, cc*LQ + qt*128: cc*LQ + (qt+1)*128],
                                 kh[:, cc*L2 + kc*512: cc*L2 + (kc+1)*512],
                                 start=(cc == 0), stop=(cc == 1))
                    lgm = t.n
                    v.wait(tsem, lgm)
                    for kc in range(4):
                        v.op("reduce_max", nm[:, kc:kc+1], ps[kc][:], AX.X, negate=True)
                    v.op("tensor_tensor", nmx[:], nm[:, 0:1], nm[:, 1:2], ALU.min)
                    v.op("tensor_tensor", nmx[:], nmx[:], nm[:, 2:3], ALU.min)
                    v.op("tensor_tensor", nmx[:], nmx[:], nm[:, 3:4], ALU.min)
                    rmx = v.n
                    s.wait(vsem, rmx)
                    s.wait(tsem, pmark.get("attn_cons", 0))
                    for kc in range(4):
                        s.op("activation", attn[:, kc*512:(kc+1)*512], ps[kc][:], ACT.Exp,
                             bias=nmx[:], accum_out=ac4[:, kc:kc+1])
                    exm = s.n
                    v.wait(ssem, exm)
                    v.op("tensor_tensor", ssum[:], ac4[:, 0:1], ac4[:, 1:2], ALU.add)
                    v.op("tensor_tensor", ssum[:], ssum[:], ac4[:, 2:3], ALU.add)
                    v.op("tensor_tensor", ssum[:], ssum[:], ac4[:, 3:4], ALU.add)
                    v.op("reciprocal", rr[:], ssum[:])
                    t.wait(ssem, exm)
                    for rnd in range(2):
                        bank = (psQ if rnd == 0 else psQ2)[:].bitcast(BF16)
                        t.wait(vsem, pmark.get(("trb", rnd), 0))
                        for k in range(8):
                            kc = rnd * 8 + k
                            t.op("transpose", bank[:, k*128:(k+1)*128], attn[:, kc*128:(kc+1)*128], idn_sb[:])
                        ttm = t.n
                        v.wait(tsem, ttm)
                        v.op("tensor_copy", attn_T[:, rnd*1024:(rnd+1)*1024], bank[:])
                        pmark[("trb", rnd)] = v.n
                    atm = v.n
                    t.wait(vsem, atm)
                    t.wait(vsem, pmark.get("ostg_cons", 0))
                    for kc in range(16):
                        t.op("matmul", ps[5][:, 0:HD], attn_T[:, kc*128:(kc+1)*128],
                             vh[:, kc*HD:(kc+1)*HD], start=(kc == 0), stop=(kc == 15))
                    avm = t.n
                    opar = qt % 2
                    v.wait(tsem, avm)
                    v.wait(sto[opar].sem, pmark.get(("ost", opar), 0))
                    v.op("tensor_scalar", ostg[opar][:], ps[5][:, 0:HD], rr[:], None, ALU.mult)
                    pmark["ostg_cons"] = v.n
                    osm = v.n
                    qwait(nc.gpsimd, vsem, osm)
                    sto[opar].dma(s_attn[qt*128:(qt+1)*128, h*HD:(h+1)*HD], ostg[opar][:])
                    pmark[("ost", opar)] = sto[opar].cnt
                    pmark["attn_cons"] = t.n
                    if qt % 2 == 1:
                        qtask(1)
                pmark["kh_cons"] = t.n
            qtask(len(wq_tiles))
            p2_marks = [(vsem, v.n), (ssem, s.n), (tsem, t.n),
                        (ldw[0].sem, ldw[0].cnt), (ldw[1].sem, ldw[1].cnt), (ld2s.sem, ld2s.cnt),
                        (ldc[0].sem, ldc[0].cnt), (ldc[1].sem, ldc[1].cnt),
                        (sto[0].sem, sto[0].cnt), (sto[1].sem, sto[1].cnt),
                        (qld[0].sem, qld[0].cnt), (qld[1].sem, qld[1].cnt),
                        (qst[0].sem, qst[0].cnt), (qst[1].sem, qst[1].cnt)]

        p3 = ExitStack()
        with p3:
            def sb3(name, shape, dt=F32):
                return p3.enter_context(nc.sbuf_tensor(name, shape, dt))
            win = [sb3("win%d" % i, [128, 2752], F32) for i in range(2)]
            QT3 = {
                "am": sb3("p_am", [128, 86], F32), "b1": sb3("p_b1", [128, 86], F32),
                "b2": sb3("p_b2", [128, 86], F32), "b3": sb3("p_b3", [128, 86], F32),
                "b4": sb3("p_b4", [128, 86], F32), "b5": sb3("p_b5", [128, 86], F32),
                "zt": sb3("p_zt", [128, 2752], F32), "at": sb3("p_at", [128, 2752], F32),
                "u2": sb3("p_u2", [128, 2752], F32),
                "cA": sb3("p_cA", [128, 2752], BF16), "cB": sb3("p_cB", [128, 2752], BF16),
                "cC": sb3("p_cC", [128, 2752], BF16), "cD": sb3("p_cD", [128, 2752], BF16),
                "cE": sb3("p_cE", [128, 2752], BF16), "cF": sb3("p_cF", [128, 2752], BF16),
                "sgn": sb3("p_sgn", [128, 2752], BF16),
            }
            wqrt = sb3("wqrt", [128, I], BF16)
            tstg3 = [sb3("tstg3%d" % i, [128, 1024], BF16) for i in range(2)]
            psT3 = [p3.enter_context(nc.psum_tensor("psT3%d" % i, [128, 1024], BF16)) for i in range(2)]

            barrier(p2_marks)
            in_free = [0, 0]
            stg_free = [0, 0]
            psT3_free = [0, 0]
            qcons = 0

            for wdram, wscr, R, Cw in ((s_wgq, s_wgT, I, D), (s_wuq, s_wuT, I, D), (wd, s_wdT, D, I)):
                prequant = wdram is not wd
                csz = 2752 if Cw % 2752 == 0 else 2048
                wscr_v = wscr.rearrange("(a p) c -> p a c", p=128)
                qwait(nc.sync, tsem, qcons)
                for rt in range(R // 128):
                    if prequant:
                        par = rt % 2
                        qwait(nc.sync, tsem, qcons)
                        ldm = ld3[par].dma(wqrt[:, 0:Cw], wdram[rt*128:(rt+1)*128, :])
                        t.wait(ld3[par].sem, ldm)
                    else:
                        for cc in range(Cw // csz):
                            par = (rt * (Cw // csz) + cc) % 2
                            qwait(nc.sync, vsem, in_free[par])
                            ldm = ld3[par].dma(win[par][:, 0:csz], wdram[rt*128:(rt+1)*128, cc*csz:(cc+1)*csz])
                            v.wait(ld3[par].sem, ldm)
                            s.wait(ld3[par].sem, ldm)
                            def qg(qcons=qcons):
                                v.wait(tsem, qcons)
                            _emit_quant(v, s, win[par][:, 0:csz], csz, 1, QT3,
                                        wqrt[:, cc*csz:(cc+1)*csz], qgate=qg)
                            in_free[par] = v.n
                        qdone = v.n
                        t.wait(vsem, qdone)
                    nk = Cw // 128
                    for batch in range((nk + 7) // 8):
                        bpar = batch % 2
                        t.wait(vsem, psT3_free[bpar])
                        klo = batch * 8
                        khi = min(klo + 8, nk)
                        for k in range(klo, khi):
                            t.op("transpose", psT3[bpar][:, (k-klo)*128:(k-klo+1)*128],
                                 wqrt[:, k*128:(k+1)*128], idn_sb[:])
                        tb = t.n
                        v.wait(tsem, tb)
                        v.wait(st3[bpar].sem, stg_free[bpar])
                        v.op("tensor_copy", tstg3[bpar][:, 0:(khi-klo)*128], psT3[bpar][:, 0:(khi-klo)*128])
                        psT3_free[bpar] = v.n
                        em = v.n
                        qwait(nc.gpsimd, vsem, em)
                        st3[bpar].dma(wscr_v[:, klo:khi, rt*128:(rt+1)*128],
                                      tstg3[bpar][:, 0:(khi-klo)*128].rearrange("p (a c) -> p a c", a=khi-klo))
                        stg_free[bpar] = st3[bpar].cnt
                    qcons = t.n
            p3_marks = [(vsem, v.n), (ssem, s.n), (tsem, t.n),
                        (ld3[0].sem, ld3[0].cnt), (ld3[1].sem, ld3[1].cnt),
                        (st3[0].sem, st3[0].cnt), (st3[1].sem, st3[1].cnt),
                        (sto[0].sem, sto[0].cnt), (sto[1].sem, sto[1].cnt)]

        p4 = ExitStack()
        with p4:
            def sb4(name, shape, dt=F32):
                return p4.enter_context(nc.sbuf_tensor(name, shape, dt))
            NIC = 22
            ic_sizes = [512] * 21 + [256]
            xnq_tt = sb4("xnq_tt", [128, 32 * 256], BF16)
            wsl = [sb4("wsl%d" % i, [128, 32 * 512], BF16) for i in range(2)]
            sgt = sb4("sgt", [128, 1024], F32)
            intf = sb4("intf", [128, 1024], F32)
            iq = sb4("iq", [128, 1024], BF16)
            iqT = sb4("iqT", [128, 86 * 256], BF16)
            wdsl = [sb4("wdsl%d" % i, [128, 512], BF16) for i in range(2)]
            xres = sb4("xres", [128, 512], F32)
            ares = sb4("ares", [128, 512], F32)
            yst = [sb4("yst%d" % i, [128, 512], F32) for i in range(2)]
            psA = p4.enter_context(nc.psum_tensor("psA", [128, 512], F32))
            psB = p4.enter_context(nc.psum_tensor("psB", [128, 512], F32))
            psC = p4.enter_context(nc.psum_tensor("psC", [128, 512], F32))
            psDD = p4.enter_context(nc.psum_tensor("psDD", [128, 512], F32))
            psTr = p4.enter_context(nc.psum_tensor("psTr", [128, 1024], BF16))
            psD = [p4.enter_context(nc.psum_tensor("psD%d" % i, [128, 512], F32)) for i in range(2)]

            barrier(p3_marks)
            wgT_v = s_wgT.rearrange("(a p) c -> p a c", p=128)
            wuT_v = s_wuT.rearrange("(a p) c -> p a c", p=128)
            xnqT_v = s_xnqT.rearrange("p (a c) -> p a c", a=32)
            gu_free = [0, 0]
            ldg_mark = [0, 0]
            iq_free = 0
            psTr_free = 0
            iqT_free = 0
            wd_free = [0, 0]
            ldd_mark = [0, 0]
            yst_free = [0, 0]
            res_free = 0
            xq_ld = 0
            ev_marks = {"sil": 0, "mul": 0}
            slab_idx = 0

            for tt2 in range(4):
                qwait(nc.sync, tsem, iqT_free)
                xq_ld = ldq4.dma(xnq_tt[:].rearrange("p (a c) -> p a c", a=32),
                                 xnqT_v[:, :, tt2*256:(tt2+1)*256])
                ioff = 0
                for ici in range(NIC):
                    isz = ic_sizes[ici]
                    pars = []
                    for wview in (wgT_v, wuT_v):
                        parw = slab_idx % 2
                        slab_idx += 1
                        pars.append(parw)
                        qwait(nc.sync, tsem, gu_free[parw])
                        ldg_mark[parw] = ldg[parw].dma(
                            wsl[parw][:, 0:32*isz].rearrange("p (a c) -> p a c", a=32),
                            wview[:, :, ioff:ioff+isz])
                    t.wait(ldq4.sem, xq_ld)
                    for gi, (parw, p0, p1) in enumerate(((pars[0], psA, psB), (pars[1], psC, psDD))):
                        t.wait(ldg[parw].sem, ldg_mark[parw])
                        t.wait(ssem, ev_marks["sil"])
                        t.wait(vsem, ev_marks["mul"])
                        for tc in range(2):
                            pp = p0 if tc == 0 else p1
                            for dc in range(32):
                                t.op("matmul", pp[:, 0:isz],
                                     xnq_tt[:, dc*256 + tc*128: dc*256 + (tc+1)*128],
                                     wsl[parw][:, dc*isz:(dc+1)*isz],
                                     start=(dc == 0), stop=(dc == 31))
                        gu_free[parw] = t.n
                    gum = t.n
                    s.wait(tsem, gum)
                    s.op("activation", sgt[:, 0:isz], psA[:, 0:isz], ACT.Silu)
                    s.op("activation", sgt[:, 512:512+isz], psB[:, 0:isz], ACT.Silu)
                    ev_marks["sil"] = s.n
                    v.wait(ssem, s.n)
                    v.op("tensor_tensor", intf[:, 0:isz], sgt[:, 0:isz], psC[:, 0:isz], ALU.mult)
                    v.op("tensor_tensor", intf[:, isz:2*isz], sgt[:, 512:512+isz], psDD[:, 0:isz], ALU.mult)
                    ev_marks["mul"] = v.n
                    def qg4():
                        v.wait(tsem, iq_free)
                    _emit_quant(v, s, intf[:, 0:2*isz], 2 * isz, 3, QTG, iq[:, 0:2*isz], qgate=qg4)
                    iqm = v.n
                    t.wait(vsem, iqm)
                    t.wait(vsem, psTr_free)
                    nk = isz // 128
                    for k in range(nk):
                        for tc in range(2):
                            t.op("transpose", psTr[:, (k*2+tc)*128:(k*2+tc+1)*128],
                                 iq[:, tc*isz + k*128: tc*isz + (k+1)*128], idn_sb[:])
                    iq_free = t.n
                    trm = t.n
                    v.wait(tsem, trm)
                    v.op("tensor_copy", iqT[:, (ioff//128)*256:(ioff//128)*256 + nk*256], psTr[:, 0:nk*256])
                    psTr_free = v.n
                    ioff += isz
                iqT_done = v.n
                t.wait(vsem, iqT_done)
                for dcD in range(8):
                    for j in range(86):
                        parw = j % 2
                        qwait(nc.sync, tsem, wd_free[parw])
                        ldd_mark[parw] = ldd[parw].dma(wdsl[parw][:],
                                                       s_wdT[j*128:(j+1)*128, dcD*512:(dcD+1)*512])
                        t.wait(ldd[parw].sem, ldd_mark[parw])
                        st_ = (j == 0); sp_ = (j == 85)
                        for tc in range(2):
                            t.op("matmul", psD[tc][:], iqT[:, j*256 + tc*128: j*256 + (tc+1)*128],
                                 wdsl[parw][:], start=st_, stop=sp_)
                        wd_free[parw] = t.n
                    dnm = t.n
                    for tc in range(2):
                        row0 = tt2 * 256 + tc * 128
                        qwait(nc.sync, vsem, res_free)
                        ldr.dma(xres[:], xb[row0:row0+128, dcD*512:(dcD+1)*512])
                        l1 = ldr.dma(ares[:], s_attn[row0:row0+128, dcD*512:(dcD+1)*512])
                        ypar = tc
                        v.wait(tsem, dnm)
                        v.wait(ldr.sem, l1)
                        v.wait(sty[ypar].sem, yst_free[ypar])
                        v.op("tensor_tensor", yst[ypar][:], psD[tc][:], xres[:], ALU.add)
                        v.op("tensor_tensor", yst[ypar][:], yst[ypar][:], ares[:], ALU.add)
                        res_free = v.n
                        ym = v.n
                        qwait(nc.gpsimd, vsem, ym)
                        sty[ypar].dma(y[row0:row0+128, dcD*512:(dcD+1)*512], yst[ypar][:])
                        yst_free[ypar] = sty[ypar].cnt
                    t.wait(vsem, res_free)
                iqT_free = t.n
            nc.sync.wait_ge(sty[0].sem, sty[0].cnt)
            nc.sync.wait_ge(sty[1].sem, sty[1].cnt)
    return nc


_NC_CACHE = None


def kernel(L0_attn_x, L0_attn_Wq, L0_attn_bq, L0_attn_Wk, L0_attn_bk,
           L0_attn_Wv, L0_attn_bv, L0_attn_bias_weight,
           L0_ffn_w_gate, L0_ffn_w_up, L0_ffn_w_down):
    global _NC_CACHE
    x = np.asarray(L0_attn_x, dtype=np.float32)
    wq = np.asarray(L0_attn_Wq, dtype=np.float32)
    wk = np.asarray(L0_attn_Wk, dtype=np.float32)
    wv = np.asarray(L0_attn_Wv, dtype=np.float32)
    bq = np.asarray(L0_attn_bq, dtype=np.float32)
    bk = np.asarray(L0_attn_bk, dtype=np.float32)
    bv = np.asarray(L0_attn_bv, dtype=np.float32)
    bw = np.asarray(L0_attn_bias_weight, dtype=np.float32)
    wg = np.asarray(L0_ffn_w_gate, dtype=np.float32)
    wu = np.asarray(L0_ffn_w_up, dtype=np.float32)
    wd = np.asarray(L0_ffn_w_down, dtype=np.float32)

    wqT = np.ascontiguousarray(wq.T)
    wkT = np.ascontiguousarray(wk.T)
    wvT = np.ascontiguousarray(wv.T)
    bq_cm = np.ascontiguousarray(bq.reshape(32, 128).T)
    bk_cm = np.ascontiguousarray(bk.reshape(32, 128).T)
    bvf = np.ascontiguousarray(np.broadcast_to(bv.reshape(1, D), (128, D)))
    bwf = np.ascontiguousarray(np.broadcast_to(bw.reshape(1, 48), (128, 48)))
    tmm = _build_tmats()
    idn = np.eye(128, dtype=ml_dtypes.bfloat16)

    if _NC_CACHE is None:
        _NC_CACHE = build_nc()
    nc = _NC_CACHE

    in_maps = []
    for c in range(8):
        b, hh = c // 2, c % 2
        own = x[b, hh*1024:(hh+1)*1024]
        oth = x[b, (1-hh)*1024:(2-hh)*1024]
        xb_core = np.ascontiguousarray(np.concatenate([own, oth], axis=0))
        in_maps.append({
            "xb": xb_core, "wqT": wqT, "wkT": wkT, "wvT": wvT,
            "bq_cm": bq_cm, "bk_cm": bk_cm, "bvf": bvf, "bwf": bwf,
            "tm": tmm, "idn": idn, "wg": wg, "wu": wu, "wd": wd,
        })
    res = run_bass_kernel_spmd(nc, in_maps, core_ids=list(range(8)))
    out = np.empty((4, 2048, 4096), dtype=np.float32)
    for c in range(8):
        b, hh = c // 2, c % 2
        out[b, hh*1024:(hh+1)*1024] = res.results[c]["y"]
    return out


# version 07 (final)
# speedup vs baseline: 1.2855x; 1.2855x over previous
import numpy as np
import ml_dtypes
import concourse.bass as bass
import concourse.mybir as mybir
from concourse.bass_utils import run_bass_kernel_spmd
from contextlib import ExitStack

F32 = mybir.dt.float32
BF16 = mybir.dt.bfloat16
ALU = mybir.AluOpType
ACT = mybir.ActivationFunctionType
AX = mybir.AxisListType

D = 4096
L2 = 2048
LQ = 1024
NH = 16
HD = 256
I = 11008
NTILE = L2 // 128
NOWN = LQ // 128
EPS = 1e-5


class Eng:
    def __init__(self, eng, sem):
        self.e = eng
        self.sem = sem
        self.n = 0

    def op(self, method, *args, **kw):
        if self.n > 0:
            self.e.wait_ge(self.sem, self.n)
        inst = getattr(self.e, method)(*args, **kw)
        inst.then_inc(self.sem, 1)
        self.n += 1
        return inst

    def wait(self, sem, count):
        if count > 0:
            self.e.wait_ge(sem, count)


class Stream:
    def __init__(self, q, sem):
        self.q = q
        self.sem = sem
        self.cnt = 0

    def dma(self, out, in_):
        self.q.dma_start(out=out, in_=in_).then_inc(self.sem, 16)
        self.cnt += 16
        return self.cnt

    def dmaT(self, out, in_):
        self.q.dma_start_transpose(out=out, in_=in_).then_inc(self.sem, 16)
        self.cnt += 16
        return self.cnt


def _build_tmats():
    g = [1.0, 1.0, 1.0]
    dim, n = 3, 8
    masks = sorted(range(n), key=lambda m: (bin(m).count('1'), m))
    inv = {m: i for i, m in enumerate(masks)}
    C = np.zeros((n, n, n), dtype=np.float64)
    for i, a in enumerate(masks):
        for j, b in enumerate(masks):
            sw, at = 0, a >> 1
            while at:
                sw += bin(at & b).count('1')
                at >>= 1
            s = -1.0 if sw & 1 else 1.0
            for t in range(dim):
                if a >> t & 1 and b >> t & 1:
                    s *= g[t]
            C[i, j, inv[a ^ b]] = s
    grades = np.array([bin(m).count('1') for m in masks])
    rev = np.where(grades * (grades - 1) // 2 % 2 == 1, -1.0, 1.0)
    BIV = [4, 5, 6]
    out = np.zeros((128, 512), dtype=np.float32)
    for c in range(3):
        G = rev[:, None] * C[:, :, BIV[c]]
        for d in range(16):
            out[d*8:(d+1)*8, c*128 + d*8:c*128 + (d+1)*8] = G
    C0 = np.diag(rev / 16.0)
    for d in range(16):
        out[d*8:(d+1)*8, 384 + d*8:384 + (d+1)*8] = C0
    return out


def _blk3d(ap, b=32):
    return ap.rearrange("p (a b) -> p a b", b=b)


def _bcast(ap, nblk):
    return ap.unsqueeze(2).broadcast_to([128, nblk, 32])


def _emit_quant(v, s, src, cols, mbits, T, qout, qgate=None):
    nb = cols // 32
    am, b1, b2, b3, b4, b5 = (T[k] for k in ("am", "b1", "b2", "b3", "b4", "b5"))
    zt, at_, u2 = T["zt"], T["at"], T["u2"]
    cA, cB, cC, cD, cE, cF = (T[k] for k in ("cA", "cB", "cC", "cD", "cE", "cF"))
    sgn = T["sgn"]
    amc = am[:, :nb]; b1c = b1[:, :nb]; b2c = b2[:, :nb]
    b3c = b3[:, :nb]; b4c = b4[:, :nb]; b5c = b5[:, :nb]
    srcc = src[:, :cols]; ztc = zt[:, :cols]; atc = at_[:, :cols]
    u2c = u2[:, :cols]
    cAc = cA[:, :cols]; cBc = cB[:, :cols]; cCc = cC[:, :cols]
    cDc = cD[:, :cols]; cEc = cE[:, :cols]; cFc = cF[:, :cols]
    sgnc = sgn[:, :cols]; qoutc = qout[:, :cols]

    v.op("tensor_reduce", amc.unsqueeze(2), _blk3d(srcc), AX.X, ALU.max,
         apply_absolute_value=True)
    v.op("tensor_single_scalar", b1c, amc, 0.0, ALU.is_gt)
    v.op("tensor_scalar", b2c, b1c, -1.0, 1.0, ALU.mult, ALU.add)
    v.op("tensor_tensor", b3c, amc, b2c, ALU.add)
    v.op("tensor_single_scalar", b3c, b3c, float(2.0**32), ALU.mult)
    v.op("memset", b4c, 1.0)
    v.op("memset", b2c, 1.0)
    for k in (16, 16, 16, 8, 4, 2, 1):
        v.op("tensor_single_scalar", b1c, b3c, float(2.0**k), ALU.is_ge)
        v.op("tensor_scalar", b5c, b1c, float(2.0**k - 1.0), 1.0, ALU.mult, ALU.add)
        v.op("tensor_tensor", b2c, b2c, b5c, ALU.mult)
        v.op("tensor_scalar", b1c, b1c, float(2.0**-k - 1.0), 1.0, ALU.mult, ALU.add)
        v.op("tensor_tensor", b3c, b3c, b1c, ALU.mult)
        v.op("tensor_tensor", b4c, b4c, b1c, ALU.mult)
    v.op("tensor_single_scalar", b4c, b4c, float(2.0**34), ALU.mult)
    v.op("tensor_single_scalar", b5c, b2c, float(2.0**-34), ALU.mult)
    v.op("tensor_tensor", _blk3d(ztc), _blk3d(srcc), _bcast(b4c, nb), ALU.mult)
    zmark = v.n
    s.wait(v.sem, zmark)
    s.op("sign", sgnc, ztc)
    if mbits == 1:
        s.op("activation", atc, ztc, ACT.Abs)
    else:
        s.op("activation", atc, ztc, ACT.Abs, scale=8.0)
    amark = s.n
    v.wait(s.sem, amark)
    if mbits == 1:
        v.op("tensor_single_scalar", cAc, atc, 0.25, ALU.is_gt)
        v.op("scalar_tensor_tensor", cAc, atc, 0.75, cAc, ALU.is_ge, ALU.add)
        v.op("scalar_tensor_tensor", cAc, atc, 1.25, cAc, ALU.is_gt, ALU.add)
        v.op("scalar_tensor_tensor", cAc, atc, 1.75, cAc, ALU.is_ge, ALU.add)
        v.op("tensor_single_scalar", cBc, atc, 2.5, ALU.is_gt)
        v.op("scalar_tensor_tensor", cBc, atc, 3.5, cBc, ALU.is_ge, ALU.add)
        v.op("tensor_single_scalar", cCc, atc, 5.0, ALU.is_gt)
        v.op("scalar_tensor_tensor", cBc, cAc, 0.5, cBc, ALU.mult, ALU.add)
        v.op("scalar_tensor_tensor", cBc, cCc, 2.0, cBc, ALU.mult, ALU.add)
        v.op("tensor_tensor", cBc, cBc, sgnc, ALU.mult)
        if qgate is not None:
            qgate()
        v.op("tensor_tensor", _blk3d(qoutc), _blk3d(cBc), _bcast(b5c, nb), ALU.mult)
    else:
        v.op("tensor_single_scalar", cAc, atc, 16.0, ALU.is_ge)
        v.op("tensor_single_scalar", cBc, atc, 32.0, ALU.is_ge)
        v.op("tensor_scalar", cCc, cAc, -0.5, 1.0, ALU.mult, ALU.add)
        v.op("scalar_tensor_tensor", cCc, cBc, -0.25, cCc, ALU.mult, ALU.add)
        v.op("tensor_tensor", u2c, atc, cCc, ALU.mult)
        v.op("tensor_single_scalar", cDc, u2c, 8.0, ALU.is_ge)
        v.op("scalar_tensor_tensor", u2c, cDc, -8.0, u2c, ALU.mult, ALU.add)
        v.op("tensor_single_scalar", cEc, u2c, 4.0, ALU.is_ge)
        v.op("scalar_tensor_tensor", u2c, cEc, -4.0, u2c, ALU.mult, ALU.add)
        v.op("tensor_single_scalar", cCc, u2c, 0.5, ALU.is_gt)
        v.op("scalar_tensor_tensor", cCc, u2c, 1.5, cCc, ALU.is_ge, ALU.add)
        v.op("scalar_tensor_tensor", cCc, u2c, 2.5, cCc, ALU.is_gt, ALU.add)
        v.op("scalar_tensor_tensor", cCc, u2c, 3.5, cCc, ALU.is_ge, ALU.add)
        v.op("scalar_tensor_tensor", cCc, cEc, 4.0, cCc, ALU.mult, ALU.add)
        v.op("scalar_tensor_tensor", cCc, cDc, 8.0, cCc, ALU.mult, ALU.add)
        v.op("tensor_scalar", cFc, cBc, 2.0, 1.0, ALU.mult, ALU.add)
        v.op("tensor_tensor", cFc, cFc, cAc, ALU.add)
        v.op("scalar_tensor_tensor", cDc, cCc, 0.125, cFc, ALU.mult, ALU.mult)
        v.op("tensor_single_scalar", cDc, cDc, 7.5, ALU.min)
        v.op("tensor_tensor", cDc, cDc, sgnc, ALU.mult)
        if qgate is not None:
            qgate()
        v.op("tensor_tensor", _blk3d(qoutc), _blk3d(cDc), _bcast(b5c, nb), ALU.mult)


def build_nc():
    nc = bass.Bass("TRN2", target_bir_lowering=False)
    xb = nc.dram_tensor("xb", [L2, D], F32, kind="ExternalInput").ap()
    wqT = nc.dram_tensor("wqT", [D, D], F32, kind="ExternalInput").ap()
    wkT = nc.dram_tensor("wkT", [D, D], F32, kind="ExternalInput").ap()
    wvT = nc.dram_tensor("wvT", [D, D], F32, kind="ExternalInput").ap()
    bq_cm = nc.dram_tensor("bq_cm", [128, 32], F32, kind="ExternalInput").ap()
    bk_cm = nc.dram_tensor("bk_cm", [128, 32], F32, kind="ExternalInput").ap()
    bvf = nc.dram_tensor("bvf", [128, D], F32, kind="ExternalInput").ap()
    bwf = nc.dram_tensor("bwf", [128, 48], F32, kind="ExternalInput").ap()
    tm = nc.dram_tensor("tm", [128, 512], F32, kind="ExternalInput").ap()
    idn = nc.dram_tensor("idn", [128, 128], BF16, kind="ExternalInput").ap()
    wg = nc.dram_tensor("wg", [I, D], F32, kind="ExternalInput").ap()
    wu = nc.dram_tensor("wu", [I, D], F32, kind="ExternalInput").ap()
    wd = nc.dram_tensor("wd", [D, I], F32, kind="ExternalInput").ap()
    y = nc.dram_tensor("y", [LQ, D], F32, kind="ExternalOutput").ap()
    s_xnT = nc.dram_tensor("s_xnT", [128, 32 * L2], BF16, kind="Internal").ap()
    s_xnqT = nc.dram_tensor("s_xnqT", [128, 32 * LQ], BF16, kind="Internal").ap()
    s_attn = nc.dram_tensor("s_attn", [LQ, D], F32, kind="Internal").ap()
    s_wgT = nc.dram_tensor("s_wgT", [D, I], BF16, kind="Internal").ap()
    s_wuT = nc.dram_tensor("s_wuT", [D, I], BF16, kind="Internal").ap()
    s_wdT = nc.dram_tensor("s_wdT", [I, D], BF16, kind="Internal").ap()

    ctx = ExitStack()
    with ctx:
        def sb(name, shape, dt=F32):
            return ctx.enter_context(nc.sbuf_tensor(name, shape, dt))

        def sem(name):
            return ctx.enter_context(nc.semaphore(name))

        vsem = sem("vsem"); ssem = sem("ssem"); tsem = sem("tsem")
        v = Eng(nc.vector, vsem)
        s = Eng(nc.scalar, ssem)
        t = Eng(nc.tensor, tsem)

        lds = Stream(nc.sync, sem("lds"))
        ldx = [Stream(nc.sync, sem("ldx%d" % i)) for i in range(2)]
        stt = [Stream(nc.gpsimd, sem("stt%d" % i)) for i in range(2)]
        ld2s = Stream(nc.sync, sem("ld2s"))
        ldw = [Stream(nc.sync, sem("ldw%d" % i)) for i in range(2)]
        ldc = [Stream(nc.sync, sem("ldc%d" % i)) for i in range(2)]
        sto = [Stream(nc.gpsimd, sem("sto%d" % i)) for i in range(2)]
        ld3 = [Stream(nc.sync, sem("ld3%d" % i)) for i in range(2)]
        st3 = [Stream(nc.gpsimd, sem("st3%d" % i)) for i in range(2)]
        ldq4 = Stream(nc.sync, sem("ldq4"))
        ldg = [Stream(nc.sync, sem("ldg%d" % i)) for i in range(2)]
        ldd = [Stream(nc.sync, sem("ldd%d" % i)) for i in range(2)]
        ldr = Stream(nc.sync, sem("ldr"))
        sty = [Stream(nc.gpsimd, sem("sty%d" % i)) for i in range(2)]

        def qwait(q, semh, val):
            if val > 0:
                q.wait_ge(semh, val)

        def barrier(marks):
            for q in (nc.sync, nc.gpsimd):
                for semh, val in marks:
                    qwait(q, semh, val)
            for e in (v, s, t):
                for semh, val in marks:
                    e.wait(semh, val)

        idn_sb = sb("idn_sb", [128, 128], BF16)
        bq_sb = sb("bq_sb", [128, 32], F32)
        bk_sb = sb("bk_sb", [128, 32], F32)
        lnS = sb("lnS", [128, 1], F32); lnQ = sb("lnQ", [128, 1], F32)
        lnM = sb("lnM", [128, 1], F32); lnE = sb("lnE", [128, 1], F32)
        lnV = sb("lnV", [128, 1], F32); lnR = sb("lnR", [128, 1], F32)
        lnB = sb("lnB", [128, 1], F32)
        nm = sb("nm", [128, 4], F32); nmx = sb("nmx", [128, 1], F32)
        ac4 = sb("ac4", [128, 4], F32); ssum = sb("ssum", [128, 1], F32)
        rr = sb("rr", [128, 1], F32)

        lds.dma(idn_sb[:], idn)
        lds.dma(bq_sb[:], bq_cm)
        lds.dma(bk_sb[:], bk_cm)
        small_ld = lds.cnt

        p1 = ExitStack()
        with p1:
            def sb1(name, shape, dt=F32):
                return p1.enter_context(nc.sbuf_tensor(name, shape, dt))
            xt2 = [sb1("xt2_%d" % i, [128, D], F32) for i in range(2)]
            sqscr = sb1("sqscr", [128, D], F32)
            xnf = sb1("xnf", [128, D], F32)
            xnb = sb1("xnb", [128, D], BF16)
            xnq_t = sb1("xnq_t", [128, D], BF16)
            tstage = [sb1("tstage%d" % i, [128, 1024], BF16) for i in range(2)]
            psT = [p1.enter_context(nc.psum_tensor("psT%d" % i, [128, 1024], BF16)) for i in range(2)]
            QT = {
                "am": sb1("q_am", [128, 64], F32), "b1": sb1("q_b1", [128, 64], F32),
                "b2": sb1("q_b2", [128, 64], F32), "b3": sb1("q_b3", [128, 64], F32),
                "b4": sb1("q_b4", [128, 64], F32), "b5": sb1("q_b5", [128, 64], F32),
                "zt": sb1("q_zt", [128, 2048], F32), "at": sb1("q_at", [128, 2048], F32),
                "u2": sb1("q_u2", [128, 2048], F32),
                "cA": sb1("q_cA", [128, 2048], BF16), "cB": sb1("q_cB", [128, 2048], BF16),
                "cC": sb1("q_cC", [128, 2048], BF16), "cD": sb1("q_cD", [128, 2048], BF16),
                "cE": sb1("q_cE", [128, 2048], BF16), "cF": sb1("q_cF", [128, 2048], BF16),
                "sgn": sb1("q_sgn", [128, 2048], BF16),
            }

            t.wait(lds.sem, small_ld)
            xnT_v = s_xnT.rearrange("p (a c) -> p a c", a=32)
            xnqT_v = s_xnqT.rearrange("p (a c) -> p a c", a=32)
            buf_free = [0, 0]
            pT_free = [0, 0]
            tstg_free = [0, 0]

            def emit_transpose_out(srcbuf, dst_view, tt):
                for half in range(4):
                    hb = half % 2
                    pt = psT[hb]
                    t.wait(vsem, pT_free[hb])
                    for k in range(8):
                        dc = half * 8 + k
                        t.op("transpose", pt[:, k*128:(k+1)*128], srcbuf[:, dc*128:(dc+1)*128], idn_sb[:])
                    tm_ = t.n
                    v.wait(tsem, tm_)
                    v.wait(stt[hb].sem, tstg_free[hb])
                    v.op("tensor_copy", tstage[hb][:], pt[:])
                    pT_free[hb] = v.n
                    em = v.n
                    qwait(nc.gpsimd, vsem, em)
                    stt[hb].dma(dst_view[:, half*8:(half+1)*8, tt*128:(tt+1)*128],
                                tstage[hb][:].rearrange("p (a c) -> p a c", a=8))
                    tstg_free[hb] = stt[hb].cnt

            for tt in range(NTILE):
                par = tt % 2
                qwait(nc.sync, ssem, buf_free[par])
                ldm = ldx[par].dma(xt2[par][:], xb[tt*128:(tt+1)*128, :])
                v.wait(ldx[par].sem, ldm)
                s.wait(ldx[par].sem, ldm)
                v.op("reduce_sum", lnS[:], xt2[par][:], AX.X)
                s.op("activation", sqscr[:], xt2[par][:], ACT.Square, accum_out=lnQ[:])
                sqm = s.n
                v.wait(ssem, sqm)
                v.op("tensor_single_scalar", lnM[:], lnS[:], 1.0 / D, ALU.mult)
                v.op("tensor_single_scalar", lnE[:], lnQ[:], 1.0 / D, ALU.mult)
                v.op("tensor_tensor", lnV[:], lnM[:], lnM[:], ALU.mult)
                v.op("tensor_tensor", lnV[:], lnE[:], lnV[:], ALU.subtract)
                v.op("tensor_single_scalar", lnV[:], lnV[:], EPS, ALU.add)
                vstatm = v.n
                s.wait(vsem, vstatm)
                s.op("activation", lnR[:], lnV[:], ACT.Sqrt)
                sq2 = s.n
                v.wait(ssem, sq2)
                v.op("reciprocal", lnR[:], lnR[:])
                v.op("scalar_tensor_tensor", lnB[:], lnM[:], -1.0, lnR[:], ALU.mult, ALU.mult)
                vstat2 = v.n
                s.wait(vsem, vstat2)
                s.op("activation", xnf[:], xt2[par][:], ACT.Identity, bias=lnB[:], scale=lnR[:])
                s.op("activation", xnb[:], xnf[:], ACT.Identity)
                buf_free[par] = s.n
                xnm = s.n
                t.wait(ssem, xnm)
                emit_transpose_out(xnb, xnT_v, tt)
                if tt < NOWN:
                    v.wait(ssem, xnm)
                    for cc in range(2):
                        _emit_quant(v, s, xnf[:, cc*2048:(cc+1)*2048], 2048, 3, QT,
                                    xnq_t[:, cc*2048:(cc+1)*2048])
                    xqm = v.n
                    t.wait(vsem, xqm)
                    emit_transpose_out(xnq_t, xnqT_v, tt)
            p1_marks = [(vsem, v.n), (ssem, s.n), (tsem, t.n), (lds.sem, lds.cnt),
                        (ldx[0].sem, ldx[0].cnt), (ldx[1].sem, ldx[1].cnt),
                        (stt[0].sem, stt[0].cnt), (stt[1].sem, stt[1].cnt)]

        p2 = ExitStack()
        with p2:
            def sb2(name, shape, dt=F32):
                return p2.enter_context(nc.sbuf_tensor(name, shape, dt))
            wstage = [sb2("wstage%d" % i, [128, 2048], F32) for i in range(2)]
            tm_sb = sb2("tm_sb", [128, 512], F32)
            bw_sb = sb2("bw_sb", [128, 48], F32)
            bv_sb = sb2("bv_sb", [128, D], F32)
            th_all = sb2("th_all", [128, NH * 128], BF16)
            tmpf = sb2("tmpf", [128, 128], F32)
            wq_bf = sb2("wq_bf", [128, 32 * HD], BF16)
            wk_bf = sb2("wk_bf", [128, 32 * HD], BF16)
            wv_bf = sb2("wv_bf", [128, 32 * HD], BF16)
            xnc = [sb2("xnc%d" % i, [128, 512], BF16) for i in range(2)]
            kh = sb2("kh", [128, 2 * L2], BF16)
            vh = sb2("vh", [128, 16 * HD], BF16)
            qtmp = sb2("qtmp", [128, 2 * LQ], BF16)
            qeff = sb2("qeff", [128, 2 * LQ], BF16)
            attn = sb2("attn", [128, L2], BF16)
            attn_T = sb2("attn_T", [128, L2], BF16)
            ostg = [sb2("ostg%d" % i, [128, HD], F32) for i in range(2)]
            ps = [p2.enter_context(nc.psum_tensor("ps%d" % i, [128, 512], F32)) for i in range(6)]
            psQ = p2.enter_context(nc.psum_tensor("psQ", [128, 512], F32))
            psQ2 = p2.enter_context(nc.psum_tensor("psQ2", [128, 512], F32))

            barrier(p1_marks)
            ld2s.dma(tm_sb[:], tm)
            ld2s.dma(bw_sb[:], bwf)
            m2 = ld2s.dma(bv_sb[:], bvf)
            v.wait(ld2s.sem, m2)
            for h in range(NH):
                def bwap(c):
                    return bw_sb[:, h * 3 + c:h * 3 + c + 1]
                v.op("tensor_single_scalar", tmpf[:], tm_sb[:, 0:128], bwap(0), ALU.mult)
                v.op("scalar_tensor_tensor", tmpf[:], tm_sb[:, 128:256], bwap(1), tmpf[:], ALU.mult, ALU.add)
                v.op("scalar_tensor_tensor", tmpf[:], tm_sb[:, 256:384], bwap(2), tmpf[:], ALU.mult, ALU.add)
                v.op("tensor_tensor", th_all[:, h*128:(h+1)*128], tmpf[:], tm_sb[:, 384:512], ALU.add)

            wqT_v = wqT.rearrange("(a p) c -> p a c", p=128)
            wkT_v = wkT.rearrange("(a p) c -> p a c", p=128)
            wvT_v = wvT.rearrange("(a p) c -> p a c", p=128)

            wfree_v = [0, 0]
            wcast = {}
            wqi = 0
            xn_free = [0, 0]
            ldc_mark = [0, 0]
            pmark = {}

            for h in range(NH):
                t.wait(vsem, pmark.get(("trb", 0), 0))
                t.wait(vsem, pmark.get(("trb", 1), 0))
                t.wait(vsem, pmark.get("ostg_cons", 0))
                for wname, wview, wbuf in (("q", wqT_v, wq_bf), ("k", wkT_v, wk_bf), ("v", wvT_v, wv_bf)):
                    for quarter in range(4):
                        wp = wqi % 2
                        wqi += 1
                        qwait(nc.sync, vsem, wfree_v[wp])
                        ldm = ldw[wp].dma(wstage[wp][:].rearrange("p (a c) -> p a c", a=8),
                                          wview[:, quarter*8:(quarter+1)*8, h*HD:(h+1)*HD])
                        v.wait(ldw[wp].sem, ldm)
                        v.op("tensor_copy", wbuf[:, quarter*2048:(quarter+1)*2048], wstage[wp][:])
                        wfree_v[wp] = v.n
                    wcast[wname] = v.n
                t.wait(vsem, max(wcast.values()))
                for tc in range(4):
                    for dc in range(32):
                        par = dc % 2
                        qwait(nc.sync, tsem, xn_free[par])
                        ldc_mark[par] = ldc[par].dma(xnc[par][:],
                                                     s_xnT[:, dc*L2 + tc*512: dc*L2 + tc*512 + 512])
                        t.wait(ldc[par].sem, ldc_mark[par])
                        st = (dc == 0)
                        sp = (dc == 31)
                        for cc in range(2):
                            t.op("matmul", ps[cc][:], wk_bf[:, (dc*2+cc)*128:(dc*2+cc+1)*128],
                                 xnc[par][:], start=st, stop=sp)
                        for sub in range(4):
                            t.op("matmul", ps[2+sub][:, 0:HD], xnc[par][:, sub*128:(sub+1)*128],
                                 wv_bf[:, dc*HD:(dc+1)*HD], start=st, stop=sp)
                        if tc < 2:
                            for cc in range(2):
                                t.op("matmul", (psQ if cc == 0 else psQ2)[:],
                                     wq_bf[:, (dc*2+cc)*128:(dc*2+cc+1)*128],
                                     xnc[par][:], start=st, stop=sp)
                        xn_free[par] = t.n
                    prm = t.n
                    s.wait(tsem, prm)
                    s.wait(tsem, pmark.get("kh_cons", 0))
                    for cc in range(2):
                        s.op("activation", kh[:, cc*L2 + tc*512: cc*L2 + tc*512 + 512],
                             ps[cc][:], ACT.Identity, bias=bk_sb[:, h*2+cc:h*2+cc+1])
                    v.wait(tsem, prm)
                    for sub in range(4):
                        kc = tc * 4 + sub
                        v.op("tensor_tensor", vh[:, kc*HD:(kc+1)*HD], ps[2+sub][:, 0:HD],
                             bv_sb[:, h*HD:(h+1)*HD], ALU.add)
                    if tc < 2:
                        for cc in range(2):
                            s.op("activation", qtmp[:, cc*LQ + tc*512: cc*LQ + tc*512 + 512],
                                 (psQ if cc == 0 else psQ2)[:], ACT.Identity,
                                 bias=bq_sb[:, h*2+cc:h*2+cc+1])
                    evK = s.n
                    evV = v.n
                    t.wait(ssem, evK)
                    t.wait(vsem, evV)
                for cc in range(2):
                    for nn in range(2):
                        t.op("matmul", ps[0][:], th_all[:, h*128:(h+1)*128],
                             qtmp[:, cc*LQ + nn*512: cc*LQ + nn*512 + 512], start=True, stop=True)
                        trm = t.n
                        s.wait(tsem, trm)
                        s.op("activation", qeff[:, cc*LQ + nn*512: cc*LQ + nn*512 + 512],
                             ps[0][:], ACT.Identity)
                        t.wait(ssem, s.n)
                qeffm = s.n
                t.wait(ssem, qeffm)
                for qt in range(8):
                    for kc in range(4):
                        for cc in range(2):
                            t.op("matmul", ps[kc][:],
                                 qeff[:, cc*LQ + qt*128: cc*LQ + (qt+1)*128],
                                 kh[:, cc*L2 + kc*512: cc*L2 + (kc+1)*512],
                                 start=(cc == 0), stop=(cc == 1))
                    lgm = t.n
                    v.wait(tsem, lgm)
                    for kc in range(4):
                        v.op("reduce_max", nm[:, kc:kc+1], ps[kc][:], AX.X, negate=True)
                    v.op("tensor_tensor", nmx[:], nm[:, 0:1], nm[:, 1:2], ALU.min)
                    v.op("tensor_tensor", nmx[:], nmx[:], nm[:, 2:3], ALU.min)
                    v.op("tensor_tensor", nmx[:], nmx[:], nm[:, 3:4], ALU.min)
                    rmx = v.n
                    s.wait(vsem, rmx)
                    s.wait(tsem, pmark.get("attn_cons", 0))
                    for kc in range(4):
                        s.op("activation", attn[:, kc*512:(kc+1)*512], ps[kc][:], ACT.Exp,
                             bias=nmx[:], accum_out=ac4[:, kc:kc+1])
                    exm = s.n
                    v.wait(ssem, exm)
                    v.op("tensor_tensor", ssum[:], ac4[:, 0:1], ac4[:, 1:2], ALU.add)
                    v.op("tensor_tensor", ssum[:], ssum[:], ac4[:, 2:3], ALU.add)
                    v.op("tensor_tensor", ssum[:], ssum[:], ac4[:, 3:4], ALU.add)
                    v.op("reciprocal", rr[:], ssum[:])
                    t.wait(ssem, exm)
                    for rnd in range(2):
                        bank = (psQ if rnd == 0 else psQ2)[:].bitcast(BF16)
                        t.wait(vsem, pmark.get(("trb", rnd), 0))
                        for k in range(8):
                            kc = rnd * 8 + k
                            t.op("transpose", bank[:, k*128:(k+1)*128], attn[:, kc*128:(kc+1)*128], idn_sb[:])
                        ttm = t.n
                        v.wait(tsem, ttm)
                        v.op("tensor_copy", attn_T[:, rnd*1024:(rnd+1)*1024], bank[:])
                        pmark[("trb", rnd)] = v.n
                    atm = v.n
                    t.wait(vsem, atm)
                    t.wait(vsem, pmark.get("ostg_cons", 0))
                    for kc in range(16):
                        t.op("matmul", ps[5][:, 0:HD], attn_T[:, kc*128:(kc+1)*128],
                             vh[:, kc*HD:(kc+1)*HD], start=(kc == 0), stop=(kc == 15))
                    avm = t.n
                    opar = qt % 2
                    v.wait(tsem, avm)
                    v.wait(sto[opar].sem, pmark.get(("ost", opar), 0))
                    v.op("tensor_scalar", ostg[opar][:], ps[5][:, 0:HD], rr[:], None, ALU.mult)
                    pmark["ostg_cons"] = v.n
                    osm = v.n
                    qwait(nc.gpsimd, vsem, osm)
                    sto[opar].dma(s_attn[qt*128:(qt+1)*128, h*HD:(h+1)*HD], ostg[opar][:])
                    pmark[("ost", opar)] = sto[opar].cnt
                    pmark["attn_cons"] = t.n
                pmark["kh_cons"] = t.n
            p2_marks = [(vsem, v.n), (ssem, s.n), (tsem, t.n),
                        (ldw[0].sem, ldw[0].cnt), (ldw[1].sem, ldw[1].cnt), (ld2s.sem, ld2s.cnt),
                        (ldc[0].sem, ldc[0].cnt), (ldc[1].sem, ldc[1].cnt),
                        (sto[0].sem, sto[0].cnt), (sto[1].sem, sto[1].cnt)]

        p3 = ExitStack()
        with p3:
            def sb3(name, shape, dt=F32):
                return p3.enter_context(nc.sbuf_tensor(name, shape, dt))
            win = [sb3("win%d" % i, [128, 2752], F32) for i in range(2)]
            QT3 = {
                "am": sb3("p_am", [128, 86], F32), "b1": sb3("p_b1", [128, 86], F32),
                "b2": sb3("p_b2", [128, 86], F32), "b3": sb3("p_b3", [128, 86], F32),
                "b4": sb3("p_b4", [128, 86], F32), "b5": sb3("p_b5", [128, 86], F32),
                "zt": sb3("p_zt", [128, 2752], F32), "at": sb3("p_at", [128, 2752], F32),
                "u2": sb3("p_u2", [128, 2752], F32),
                "cA": sb3("p_cA", [128, 2752], BF16), "cB": sb3("p_cB", [128, 2752], BF16),
                "cC": sb3("p_cC", [128, 2752], BF16), "cD": sb3("p_cD", [128, 2752], BF16),
                "cE": sb3("p_cE", [128, 2752], BF16), "cF": sb3("p_cF", [128, 2752], BF16),
                "sgn": sb3("p_sgn", [128, 2752], BF16),
            }
            wqrt = sb3("wqrt", [128, I], BF16)
            tstg3 = [sb3("tstg3%d" % i, [128, 1024], BF16) for i in range(2)]
            psT3 = [p3.enter_context(nc.psum_tensor("psT3%d" % i, [128, 1024], BF16)) for i in range(2)]

            barrier(p2_marks)
            in_free = [0, 0]
            stg_free = [0, 0]
            psT3_free = [0, 0]
            qcons = 0

            for wdram, wscr, R, Cw in ((wg, s_wgT, I, D), (wu, s_wuT, I, D), (wd, s_wdT, D, I)):
                csz = 2752 if Cw % 2752 == 0 else 2048
                wscr_v = wscr.rearrange("(a p) c -> p a c", p=128)
                for rt in range(R // 128):
                    for cc in range(Cw // csz):
                        par = (rt * (Cw // csz) + cc) % 2
                        qwait(nc.sync, vsem, in_free[par])
                        ldm = ld3[par].dma(win[par][:, 0:csz], wdram[rt*128:(rt+1)*128, cc*csz:(cc+1)*csz])
                        v.wait(ld3[par].sem, ldm)
                        s.wait(ld3[par].sem, ldm)
                        def qg(qcons=qcons):
                            v.wait(tsem, qcons)
                        _emit_quant(v, s, win[par][:, 0:csz], csz, 1, QT3,
                                    wqrt[:, cc*csz:(cc+1)*csz], qgate=qg)
                        in_free[par] = v.n
                    qdone = v.n
                    t.wait(vsem, qdone)
                    nk = Cw // 128
                    for batch in range((nk + 7) // 8):
                        bpar = batch % 2
                        t.wait(vsem, psT3_free[bpar])
                        klo = batch * 8
                        khi = min(klo + 8, nk)
                        for k in range(klo, khi):
                            t.op("transpose", psT3[bpar][:, (k-klo)*128:(k-klo+1)*128],
                                 wqrt[:, k*128:(k+1)*128], idn_sb[:])
                        tb = t.n
                        v.wait(tsem, tb)
                        v.wait(st3[bpar].sem, stg_free[bpar])
                        v.op("tensor_copy", tstg3[bpar][:, 0:(khi-klo)*128], psT3[bpar][:, 0:(khi-klo)*128])
                        psT3_free[bpar] = v.n
                        em = v.n
                        qwait(nc.gpsimd, vsem, em)
                        st3[bpar].dma(wscr_v[:, klo:khi, rt*128:(rt+1)*128],
                                      tstg3[bpar][:, 0:(khi-klo)*128].rearrange("p (a c) -> p a c", a=khi-klo))
                        stg_free[bpar] = st3[bpar].cnt
                    qcons = t.n
            p3_marks = [(vsem, v.n), (ssem, s.n), (tsem, t.n),
                        (ld3[0].sem, ld3[0].cnt), (ld3[1].sem, ld3[1].cnt),
                        (st3[0].sem, st3[0].cnt), (st3[1].sem, st3[1].cnt),
                        (sto[0].sem, sto[0].cnt), (sto[1].sem, sto[1].cnt)]

        p4 = ExitStack()
        with p4:
            def sb4(name, shape, dt=F32):
                return p4.enter_context(nc.sbuf_tensor(name, shape, dt))
            NIC = 22
            ic_sizes = [512] * 21 + [256]
            xnq_tt = sb4("xnq_tt", [128, 32 * 256], BF16)
            wsl = [sb4("wsl%d" % i, [128, 32 * 512], BF16) for i in range(2)]
            sgt = sb4("sgt", [128, 1024], F32)
            intf = sb4("intf", [128, 1024], F32)
            QT4 = {
                "am": sb4("f_am", [128, 32], F32), "b1": sb4("f_b1", [128, 32], F32),
                "b2": sb4("f_b2", [128, 32], F32), "b3": sb4("f_b3", [128, 32], F32),
                "b4": sb4("f_b4", [128, 32], F32), "b5": sb4("f_b5", [128, 32], F32),
                "zt": sb4("f_zt", [128, 1024], F32), "at": sb4("f_at", [128, 1024], F32),
                "u2": sb4("f_u2", [128, 1024], F32),
                "cA": sb4("f_cA", [128, 1024], BF16), "cB": sb4("f_cB", [128, 1024], BF16),
                "cC": sb4("f_cC", [128, 1024], BF16), "cD": sb4("f_cD", [128, 1024], BF16),
                "cE": sb4("f_cE", [128, 1024], BF16), "cF": sb4("f_cF", [128, 1024], BF16),
                "sgn": sb4("f_sgn", [128, 1024], BF16),
            }
            iq = sb4("iq", [128, 1024], BF16)
            iqT = sb4("iqT", [128, 86 * 256], BF16)
            wdsl = [sb4("wdsl%d" % i, [128, 4 * 512], BF16) for i in range(2)]
            xres = sb4("xres", [128, 512], F32)
            ares = sb4("ares", [128, 512], F32)
            yst = [sb4("yst%d" % i, [128, 512], F32) for i in range(2)]
            psA = p4.enter_context(nc.psum_tensor("psA", [128, 512], F32))
            psB = p4.enter_context(nc.psum_tensor("psB", [128, 512], F32))
            psC = p4.enter_context(nc.psum_tensor("psC", [128, 512], F32))
            psDD = p4.enter_context(nc.psum_tensor("psDD", [128, 512], F32))
            psTr = p4.enter_context(nc.psum_tensor("psTr", [128, 1024], BF16))
            psD = [p4.enter_context(nc.psum_tensor("psD%d" % i, [128, 512], F32)) for i in range(2)]

            barrier(p3_marks)
            wgT_v = s_wgT.rearrange("(a p) c -> p a c", p=128)
            wuT_v = s_wuT.rearrange("(a p) c -> p a c", p=128)
            xnqT_v = s_xnqT.rearrange("p (a c) -> p a c", a=32)
            gu_free = [0, 0]
            ldg_mark = [0, 0]
            iq_free = 0
            psTr_free = 0
            iqT_free = 0
            wd_free = [0, 0]
            ldd_mark = [0, 0]
            yst_free = [0, 0]
            res_free = 0
            xq_ld = 0
            ev_marks = {"sil": 0, "mul": 0}
            slab_idx = 0

            for tt2 in range(4):
                qwait(nc.sync, tsem, iqT_free)
                xq_ld = ldq4.dma(xnq_tt[:].rearrange("p (a c) -> p a c", a=32),
                                 xnqT_v[:, :, tt2*256:(tt2+1)*256])
                ioff = 0
                for ici in range(NIC):
                    isz = ic_sizes[ici]
                    pars = []
                    for wview in (wgT_v, wuT_v):
                        parw = slab_idx % 2
                        slab_idx += 1
                        pars.append(parw)
                        qwait(nc.sync, tsem, gu_free[parw])
                        ldg_mark[parw] = ldg[parw].dma(
                            wsl[parw][:, 0:32*isz].rearrange("p (a c) -> p a c", a=32),
                            wview[:, :, ioff:ioff+isz])
                    t.wait(ldq4.sem, xq_ld)
                    for gi, (parw, p0, p1) in enumerate(((pars[0], psA, psB), (pars[1], psC, psDD))):
                        t.wait(ldg[parw].sem, ldg_mark[parw])
                        t.wait(ssem, ev_marks["sil"])
                        t.wait(vsem, ev_marks["mul"])
                        for tc in range(2):
                            pp = p0 if tc == 0 else p1
                            for dc in range(32):
                                t.op("matmul", pp[:, 0:isz],
                                     xnq_tt[:, dc*256 + tc*128: dc*256 + (tc+1)*128],
                                     wsl[parw][:, dc*isz:(dc+1)*isz],
                                     start=(dc == 0), stop=(dc == 31))
                        gu_free[parw] = t.n
                    gum = t.n
                    s.wait(tsem, gum)
                    s.op("activation", sgt[:, 0:isz], psA[:, 0:isz], ACT.Silu)
                    s.op("activation", sgt[:, 512:512+isz], psB[:, 0:isz], ACT.Silu)
                    ev_marks["sil"] = s.n
                    v.wait(ssem, s.n)
                    v.op("tensor_tensor", intf[:, 0:isz], sgt[:, 0:isz], psC[:, 0:isz], ALU.mult)
                    v.op("tensor_tensor", intf[:, isz:2*isz], sgt[:, 512:512+isz], psDD[:, 0:isz], ALU.mult)
                    ev_marks["mul"] = v.n
                    def qg4():
                        v.wait(tsem, iq_free)
                    _emit_quant(v, s, intf[:, 0:2*isz], 2 * isz, 3, QT4, iq[:, 0:2*isz], qgate=qg4)
                    iqm = v.n
                    t.wait(vsem, iqm)
                    t.wait(vsem, psTr_free)
                    nk = isz // 128
                    for k in range(nk):
                        for tc in range(2):
                            t.op("transpose", psTr[:, (k*2+tc)*128:(k*2+tc+1)*128],
                                 iq[:, tc*isz + k*128: tc*isz + (k+1)*128], idn_sb[:])
                    iq_free = t.n
                    trm = t.n
                    v.wait(tsem, trm)
                    v.op("tensor_copy", iqT[:, (ioff//128)*256:(ioff//128)*256 + nk*256], psTr[:, 0:nk*256])
                    psTr_free = v.n
                    ioff += isz
                iqT_done = v.n
                t.wait(vsem, iqT_done)
                wdT_v = s_wdT.rearrange("(a p) c -> p a c", p=128)
                jbs = [(jb, min(jb + 4, 86)) for jb in range(0, 86, 4)]
                for dcD in range(8):
                    for bi, (jlo, jhi) in enumerate(jbs):
                        parw = bi % 2
                        nj = jhi - jlo
                        qwait(nc.sync, tsem, wd_free[parw])
                        ldd_mark[parw] = ldd[parw].dma(
                            wdsl[parw][:, 0:nj*512].rearrange("p (a c) -> p a c", a=nj),
                            wdT_v[:, jlo:jhi, dcD*512:(dcD+1)*512])
                        t.wait(ldd[parw].sem, ldd_mark[parw])
                        for j in range(jlo, jhi):
                            st_ = (j == 0); sp_ = (j == 85)
                            for tc in range(2):
                                t.op("matmul", psD[tc][:], iqT[:, j*256 + tc*128: j*256 + (tc+1)*128],
                                     wdsl[parw][:, (j-jlo)*512:(j-jlo+1)*512], start=st_, stop=sp_)
                        wd_free[parw] = t.n
                    dnm = t.n
                    for tc in range(2):
                        row0 = tt2 * 256 + tc * 128
                        qwait(nc.sync, vsem, res_free)
                        ldr.dma(xres[:], xb[row0:row0+128, dcD*512:(dcD+1)*512])
                        l1 = ldr.dma(ares[:], s_attn[row0:row0+128, dcD*512:(dcD+1)*512])
                        ypar = tc
                        v.wait(tsem, dnm)
                        v.wait(ldr.sem, l1)
                        v.wait(sty[ypar].sem, yst_free[ypar])
                        v.op("tensor_tensor", yst[ypar][:], psD[tc][:], xres[:], ALU.add)
                        v.op("tensor_tensor", yst[ypar][:], yst[ypar][:], ares[:], ALU.add)
                        res_free = v.n
                        ym = v.n
                        qwait(nc.gpsimd, vsem, ym)
                        sty[ypar].dma(y[row0:row0+128, dcD*512:(dcD+1)*512], yst[ypar][:])
                        yst_free[ypar] = sty[ypar].cnt
                    t.wait(vsem, res_free)
                iqT_free = t.n
            nc.sync.wait_ge(sty[0].sem, sty[0].cnt)
            nc.sync.wait_ge(sty[1].sem, sty[1].cnt)
    return nc


_NC_CACHE = None


def kernel(L0_attn_x, L0_attn_Wq, L0_attn_bq, L0_attn_Wk, L0_attn_bk,
           L0_attn_Wv, L0_attn_bv, L0_attn_bias_weight,
           L0_ffn_w_gate, L0_ffn_w_up, L0_ffn_w_down):
    global _NC_CACHE
    x = np.asarray(L0_attn_x, dtype=np.float32)
    wq = np.asarray(L0_attn_Wq, dtype=np.float32)
    wk = np.asarray(L0_attn_Wk, dtype=np.float32)
    wv = np.asarray(L0_attn_Wv, dtype=np.float32)
    bq = np.asarray(L0_attn_bq, dtype=np.float32)
    bk = np.asarray(L0_attn_bk, dtype=np.float32)
    bv = np.asarray(L0_attn_bv, dtype=np.float32)
    bw = np.asarray(L0_attn_bias_weight, dtype=np.float32)
    wg = np.asarray(L0_ffn_w_gate, dtype=np.float32)
    wu = np.asarray(L0_ffn_w_up, dtype=np.float32)
    wd = np.asarray(L0_ffn_w_down, dtype=np.float32)

    wqT = np.ascontiguousarray(wq.T)
    wkT = np.ascontiguousarray(wk.T)
    wvT = np.ascontiguousarray(wv.T)
    bq_cm = np.ascontiguousarray(bq.reshape(32, 128).T)
    bk_cm = np.ascontiguousarray(bk.reshape(32, 128).T)
    bvf = np.ascontiguousarray(np.broadcast_to(bv.reshape(1, D), (128, D)))
    bwf = np.ascontiguousarray(np.broadcast_to(bw.reshape(1, 48), (128, 48)))
    tmm = _build_tmats()
    idn = np.eye(128, dtype=ml_dtypes.bfloat16)

    if _NC_CACHE is None:
        _NC_CACHE = build_nc()
    nc = _NC_CACHE

    in_maps = []
    for c in range(8):
        b, hh = c // 2, c % 2
        own = x[b, hh*1024:(hh+1)*1024]
        oth = x[b, (1-hh)*1024:(2-hh)*1024]
        xb_core = np.ascontiguousarray(np.concatenate([own, oth], axis=0))
        in_maps.append({
            "xb": xb_core, "wqT": wqT, "wkT": wkT, "wvT": wvT,
            "bq_cm": bq_cm, "bk_cm": bk_cm, "bvf": bvf, "bwf": bwf,
            "tm": tmm, "idn": idn, "wg": wg, "wu": wu, "wd": wd,
        })
    res = run_bass_kernel_spmd(nc, in_maps, core_ids=list(range(8)))
    out = np.empty((4, 2048, 4096), dtype=np.float32)
    for c in range(8):
        b, hh = c // 2, c % 2
        out[b, hh*1024:(hh+1)*1024] = res.results[c]["y"]
    return out
